# version 01
# baseline (speedup 1.0000x reference)
import contextlib
import numpy as np
import ml_dtypes
import concourse.bass as bass
import concourse.mybir as mybir
from concourse.bass_utils import run_bass_kernel_spmd

F32 = mybir.dt.float32
BF16 = mybir.dt.bfloat16
I32 = mybir.dt.int32
AF = mybir.ActivationFunctionType
ALU = mybir.AluOpType
AX = mybir.AxisListType.X
AFCOPY = getattr(AF, "Copy", None) or getattr(AF, "Identity")

H = 4096
PROJ = 8512
CONV = 4352
NH = 64
KC = 32
L = 8192
NB = 8
BL = 1024
NT = 8
NCH = 34
EPS = 1e-5
FP8M = 448.0


class Ctr:
    def __init__(self):
        self.v = {}

    def bump(self, name, amt=1):
        self.v[name] = self.v.get(name, 0) + amt
        return self.v[name]

    def val(self, name):
        return self.v.get(name, 0)


CUT = None
OUTREV = False
OUTSENT = False
OUTPS = None
OUTSER = False
_STAGES = ["consts", "tile1", "tile2", "stageA", "stageB", "stageD", "phase1", "rld",
           "p2t1", "p2t2", "p2a", "p2b", "p2c", "p2d", "p2e", "p2scan",
           "zgate", "rms2", "xnt", "end"]


class _CutDone(Exception):
    pass


def build(debug=False):
    nc = bass.Bass("TRN2", target_bir_lowering=False, debug=False,
                   detect_race_conditions=False)

    x_full = nc.dram_tensor("x_full", [L, H], F32, kind="ExternalInput").ap()
    x_own = nc.dram_tensor("x_own", [BL, H], F32, kind="ExternalInput").ap()
    w_in = nc.dram_tensor("w_in", [H, PROJ], F32, kind="ExternalInput").ap()
    w_out = nc.dram_tensor("w_out", [H, H], F32, kind="ExternalInput").ap()
    w_ffn = nc.dram_tensor("w_ffn", [H, H], F32, kind="ExternalInput").ap()
    lnw_f = nc.dram_tensor("lnw_f", [128, KC], F32, kind="ExternalInput").ap()
    nrm_f = nc.dram_tensor("nrm_f", [128, KC], F32, kind="ExternalInput").ap()
    mup_f = nc.dram_tensor("mup_f", [128, NCH], F32, kind="ExternalInput").ap()
    mup_d = nc.dram_tensor("mup_d", [64, 1], F32, kind="ExternalInput").ap()
    mup_z = nc.dram_tensor("mup_z", [1, H], F32, kind="ExternalInput").ap()
    cw_f = nc.dram_tensor("cw_f", [128, NCH * 4], F32, kind="ExternalInput").ap()
    alog = nc.dram_tensor("alog", [64, 1], F32, kind="ExternalInput").ap()
    dtb = nc.dram_tensor("dtb", [64, 1], F32, kind="ExternalInput").ap()
    drow = nc.dram_tensor("drow", [1, 64], F32, kind="ExternalInput").ap()
    ident = nc.dram_tensor("ident", [128, 128], F32, kind="ExternalInput").ap()
    identb = nc.dram_tensor("identb", [128, 128], BF16, kind="ExternalInput").ap()
    caus = nc.dram_tensor("caus", [128, 128], F32, kind="ExternalInput").ap()
    ones1 = nc.dram_tensor("ones1", [1, 128], F32, kind="ExternalInput").ap()
    myoff = nc.dram_tensor("myoff", [1, 2], I32, kind="ExternalInput").ap()

    out = nc.dram_tensor("out", [BL, H], F32, kind="ExternalOutput").ap()

    sv_xrt = nc.dram_tensor("sv_xrt", [NB * 128, KC * BL], BF16, kind="Internal")
    sv_xs = nc.dram_tensor("sv_xs", [NB * 128, NT * H], BF16, kind="Internal")
    sv_bc = nc.dram_tensor("sv_bc", [NB * 128, 3 * BL], BF16, kind="Internal")
    sv_dt = nc.dram_tensor("sv_dt", [NB * 64, BL], F32, kind="Internal")
    sv_st = nc.dram_tensor("sv_st", [NB * 128, H], F32, kind="Internal")
    sv_lns = nc.dram_tensor("sv_lns", [NB * 128, 2 * NT], F32, kind="Internal")
    acd_dram = nc.dram_tensor("acd_dram", [64, 128], F32, kind="Internal")

    dbg = {}
    if debug:
        for nm, shp, dt_ in [("d_state", [128, H], F32), ("d_y", [128, NT * H], F32),
                             ("d_dt", [128, BL], F32), ("d_dtq", [128, BL], F32),
                             ("d_misc", [128, 2048], F32)]:
            dbg[nm] = nc.dram_tensor(nm, shp, dt_, kind="ExternalOutput").ap()
        for nm, shp, dt_ in [("d_raw", [128, KC * BL], BF16)]:
            dbg[nm] = nc.dram_tensor(nm, shp, dt_, kind="ExternalOutput").ap()

    c = Ctr()

    stack = contextlib.ExitStack()
    with stack:
        def sbt(name, shape, dt_):
            return stack.enter_context(nc.sbuf_tensor(name, shape, dt_))

        XRT = sbt("XRT", [128, KC * BL], BF16)
        XS = sbt("XS", [128, NT * H], BF16)
        ST = sbt("ST", [128, H], F32)
        BF = sbt("BF", [128, BL], BF16)
        CF = sbt("CF", [128, BL], BF16)
        BT8 = sbt("BT8", [128, BL], BF16)
        DTQ = sbt("DTQ", [128, BL], F32)
        DTQ2 = sbt("DTQ2", [128, BL], F32)
        XRB = sbt("XRB", [128, H], F32)
        WSA = sbt("WSA", [128, 4 * 512], BF16)
        WSB = sbt("WSB", [128, 4 * 512], BF16)
        SC1 = sbt("SC1", [128, 1024], F32)
        SC2 = sbt("SC2", [128, 1024], F32)
        TAIL = sbt("TAIL", [128, NCH * 3], F32)
        ACDT = sbt("ACDT", [128, 64], F32)
        DTT = sbt("DTT", [128, 64], F32)
        ABB = sbt("ABB", [128, 64], F32)
        FTOK = sbt("FTOK", [128, 64], F32)
        PCE = sbt("PCE", [128, 64], F32)
        ROWS = sbt("ROWS", [1, 64], F32)
        XDP = sbt("XDP", [128, 512], BF16)
        SBF = sbt("SBF", [128, H], BF16)
        STAT = sbt("STAT", [128, 16], F32)
        LNSV = sbt("LNSV", [128, 2 * NT], F32)
        AMX = sbt("AMX", [128, 1], F32)
        WMX = sbt("WMX", [128, 1], F32)
        LNW = sbt("LNW", [128, KC], F32)
        NRW = sbt("NRW", [128, KC], F32)
        MUP = sbt("MUP", [128, NCH], F32)
        MUD = sbt("MUD", [64, 1], F32)
        CW = sbt("CW", [128, NCH * 4], F32)
        AV = sbt("AV", [64, 1], F32)
        DTBS = sbt("DTBS", [64, 1], F32)
        IDT = sbt("IDT", [128, 128], F32)
        IDB = sbt("IDB", [128, 128], BF16)
        CAUS = sbt("CAUS", [128, 128], F32)
        ON1 = sbt("ON1", [1, 128], F32)
        DTOK = sbt("DTOK", [128, 64], F32)
        MYO = sbt("MYO", [1, 2], I32)
        GDB = sbt("GDB", [128, 128], BF16)
        BIAS = sbt("BIAS", [128, 64], F32)
        EACD = sbt("EACD", [128, 64], F32)
        ACB = sbt("ACB", [128, 512], F32)
        SXB = sbt("SXB", [128, 1], F32)
        SWBC = sbt("SWBC", [128, 1], F32)
        INVB = sbt("INVB", [128, 1], F32)
        SM1 = sbt("SM1", [1, 160], F32)
        YEV = SC1[:, 0:512]
        XOT = SC1[:, 512:1024]
        MZS = SC2[:, 0:512]
        GTC = SC1[:, 512:640]
        MSK = SC1[:, 640:768]

        P = [stack.enter_context(nc.psum_tensor(f"P{i}", [128, 512], F32))
             for i in range(8)]
        P0, P1, P2, P3, P4, P5, P6, P7 = P

        s_w = stack.enter_context(nc.semaphore("s_w"))
        s_x = stack.enter_context(nc.semaphore("s_x"))
        s_pe = stack.enter_context(nc.semaphore("s_pe"))
        s_dv = stack.enter_context(nc.semaphore("s_dv"))
        s_ac = stack.enter_context(nc.semaphore("s_ac"))
        s_sv = stack.enter_context(nc.semaphore("s_sv"))
        s_sg = stack.enter_context(nc.semaphore("s_sg"))
        s_st = stack.enter_context(nc.semaphore("s_st"))
        s_cv = stack.enter_context(nc.semaphore("s_cv"))
        s_ca = stack.enter_context(nc.semaphore("s_ca"))
        block = stack.enter_context(nc.Block())
        chn = {"v": 0, "a": 0}

        def vsync(e, inst):
            chn["v"] += 1
            inst.then_inc(s_cv, 1)
            e.wait_ge(s_cv, chn["v"])

        def asyncb(e, inst):
            chn["a"] += 1
            inst.then_inc(s_ca, 1)
            e.wait_ge(s_ca, chn["a"])

        DTF = DTQ[0:64, :]
        ADT = DTQ[64:128, :]
        ACD = DTQ2[0:64, :]
        LDT = DTQ2[64:128, :]

        prog = {"sync": [], "gpsimd": [], "tensor": [], "vector": [], "scalar": []}

        def emit(engine, fn):
            prog[engine].append(fn)

        def wait(engine, sem, val):
            if val > 0:
                emit(engine, lambda e, sem=sem, val=val: e.wait_ge(sem, val))

        def checkpoint(name):
            if CUT is not None and _STAGES.index(name) >= _STAGES.index(CUT):
                def fin_dma(e, wd=c.val("s_dv"), wa=c.val("s_ac"), wp=c.val("s_pe"),
                            ws=c.val("s_sg"), wv=c.val("s_sv"), wx=c.val("s_x"),
                            ww=c.val("s_w"), wst=c.val("s_st"), dbg_=debug):
                    for sem, v in [(s_dv, wd), (s_ac, wa), (s_pe, wp), (s_sg, ws),
                                   (s_sv, wv), (s_x, wx), (s_w, ww), (s_st, wst)]:
                        if v > 0:
                            e.wait_ge(sem, v)
                    n16 = 16
                    if dbg_:
                        e.dma_start(out=dbg["d_raw"], in_=XRT[:]).then_inc(s_st, 16)
                        e.dma_start(out=dbg["d_dtq"], in_=DTQ[:]).then_inc(s_st, 16)
                        e.dma_start(out=dbg["d_dt"], in_=DTQ2[:]).then_inc(s_st, 16)
                        e.dma_start(out=dbg["d_state"], in_=XS[:, 0:H]).then_inc(s_st, 16)
                        n16 += 64
                    e.dma_start(out=out[0:128, 0:512], in_=SC2[:, 0:512]).then_inc(s_st, 16)
                    e.wait_ge(s_st, wst + n16)
                emit("gpsimd", fin_dma)
                raise _CutDone()

        try:
            def p0_loads(e):
                for dst, src in [(LNW[:], lnw_f), (NRW[:], nrm_f), (MUP[:], mup_f),
                                 (MUD[:], mup_d), (CW[:], cw_f), (AV[:], alog),
                                 (DTBS[:], dtb), (IDT[:], ident), (IDB[:], identb),
                                 (CAUS[:], caus), (ON1[:], ones1), (MYO[:], myoff),
                                 (SM1[0:1, 0:64], drow)]:
                    e.dma_start(out=dst, in_=src).then_inc(s_sv, 16)
            emit("sync", p0_loads)
            c.bump("s_sv", 13 * 16)
            v_consts = c.val("s_sv")

            wait("scalar", s_sv, v_consts)
            emit("scalar", lambda e: e.activation(AV[:], AV[:], AF.Exp).then_inc(s_ac, 1))
            c.bump("s_ac")
            wait("vector", s_sv, v_consts)
            wait("vector", s_ac, c.val("s_ac"))
            def p0_init(e):
                e.memset(AMX[:], 0.0)
                e.memset(WMX[:], 0.0)
                e.memset(ST[:], 0.0)
                e.memset(TAIL[:], 0.0)
                e.tensor_scalar(AV[:], AV[:], -1.0, None, ALU.mult).then_inc(s_dv, 1)
            emit("vector", p0_init)
            c.bump("s_dv")

            wait("tensor", s_sv, v_consts)
            wait("tensor", s_dv, c.val("s_dv"))
            def p0_dtok(e):
                e.matmul(P7[0:128, 0:64], ON1[0:1, 0:128], SM1[0:1, 0:64],
                         start=True, stop=True).then_inc(s_pe, 1)
            emit("tensor", p0_dtok)
            c.bump("s_pe")
            wait("scalar", s_pe, c.val("s_pe"))
            emit("scalar", lambda e: e.copy(DTOK[:], P7[0:128, 0:64]).then_inc(s_ac, 1))
            c.bump("s_ac")

            checkpoint("consts")

            if OUTSENT:
                wait("vector", s_dv, c.val("s_dv"))
                def snt(e):
                    e.memset(SC2[:, 0:512], 12345.0).then_inc(s_dv, 1)
                emit("vector", snt)
                c.bump("s_dv")
                def sntd(e, w=c.val("s_dv")):
                    e.wait_ge(s_dv, w)
                    for j in range(NT):
                        for nbq in range(8):
                            e.dma_start(out=out[j * 128:(j + 1) * 128, nbq * 512:(nbq + 1) * 512],
                                        in_=SC2[:, 0:512]).then_inc(s_st, 16)
                emit("gpsimd", sntd)
                c.bump("s_st", NT * 8 * 16)
                wait("vector", s_st, c.val("s_st"))

            def chunk_tokens(cc, with_ldt=False):
                v_prev_ac = c.val("s_ac")
                v_prev_dv = c.val("s_dv")
                def trd(e, cc=cc, w=v_prev_ac, wd=v_prev_dv, wl=with_ldt):
                    if w > 0:
                        e.wait_ge(s_ac, w)
                    if wd > 0:
                        e.wait_ge(s_dv, wd)
                    e.transpose(P6[0:128, 0:64], ACD[:, cc * 128:(cc + 1) * 128], IDT[0:64, 0:64])
                    e.transpose(P6[0:128, 64:128], DTF[:, cc * 128:(cc + 1) * 128], IDT[0:64, 0:64])
                    e.transpose(P7[0:1, 0:64], ACD[:, cc * 128 + 127:cc * 128 + 128],
                                IDT[0:64, 0:64]).then_inc(s_pe, 1)
                emit("tensor", trd)
                c.bump("s_pe")
                wait("scalar", s_pe, c.val("s_pe"))
                def trde(e, wl=with_ldt):
                    e.copy(ACDT[:], P6[0:128, 0:64])
                    e.copy(DTT[:], P6[0:128, 64:128])
                    if wl:
                        e.activation(BIAS[:], P6[0:128, 64:128], AF.Ln)
                        e.activation(EACD[:], P6[0:128, 0:64], AF.Exp)
                    asyncb(e, e.copy(ROWS[0:1, 0:64], P7[0:1, 0:64]))
                    e.nop().then_inc(s_ac, 1)
                emit("scalar", trde)
                c.bump("s_ac")
                wait("tensor", s_ac, c.val("s_ac"))
                def bca(e):
                    e.matmul(P6[0:128, 0:64], ON1[0:1, 0:128], ROWS[0:1, 0:64],
                             start=True, stop=True).then_inc(s_pe, 1)
                emit("tensor", bca)
                c.bump("s_pe")
                wait("scalar", s_pe, c.val("s_pe"))
                def bcae(e):
                    asyncb(e, e.copy(ABB[:], P6[0:128, 0:64]))
                    e.activation(PCE[:], ABB[:], AF.Exp).then_inc(s_ac, 1)
                emit("scalar", bcae)
                c.bump("s_ac")

            def state_update(cc, extra_wait_sv=0):
                v_ab = c.val("s_ac")
                wait("vector", s_ac, v_ab)
                def fac(e):
                    e.tensor_tensor(FTOK[:], ABB[:], ACDT[:], ALU.subtract).then_inc(s_dv, 1)
                emit("vector", fac)
                c.bump("s_dv")
                wait("scalar", s_dv, c.val("s_dv"))
                emit("scalar", lambda e: e.activation(FTOK[:], FTOK[:], AF.Exp).then_inc(s_ac, 1))
                c.bump("s_ac")
                wait("vector", s_ac, c.val("s_ac"))
                def facm(e):
                    vsync(e, e.tensor_tensor(FTOK[:], FTOK[:], DTT[:], ALU.mult))
                    e.nop().then_inc(s_dv, 1)
                emit("vector", facm)
                c.bump("s_dv")
                for pg in range(8):
                    ps = P4 if pg % 2 == 0 else P5
                    v_prev_mm = c.val("s_pe")
                    def xdpb(e, cc=cc, pg=pg, w=c.val("s_dv"), wp=v_prev_mm):
                        e.wait_ge(s_dv, w)
                        if wp > 0:
                            e.wait_ge(s_pe, wp)
                        for ri in range(8):
                            r = pg * 8 + ri
                            last = e.tensor_scalar(
                                XDP[:, ri * 64:(ri + 1) * 64],
                                XS[:, cc * H + r * 64:cc * H + r * 64 + 64],
                                FTOK[:, r:r + 1], None, ALU.mult)
                        vsync(e, last)
                        e.nop().then_inc(s_dv, 1)
                    emit("vector", xdpb)
                    c.bump("s_dv")
                    v_xdp = c.val("s_dv")
                    def smm(e, cc=cc, ps=ps, w=v_xdp):
                        e.wait_ge(s_dv, w)
                        e.matmul(ps[0:128, 0:512], BT8[:, cc * 128:(cc + 1) * 128],
                                 XDP[:, 0:512], start=True, stop=True).then_inc(s_pe, 1)
                    emit("tensor", smm)
                    c.bump("s_pe")
                    v_mm = c.val("s_pe")
                    def sadd(e, pg=pg, ps=ps, w=v_mm, wa=v_ab,
                             wsv=extra_wait_sv if pg == 0 else 0):
                        e.wait_ge(s_ac, wa)
                        if wsv:
                            e.wait_ge(s_sg, wsv)
                        for r in range(pg * 8, pg * 8 + 8):
                            last = e.tensor_scalar(ST[:, r * 64:(r + 1) * 64],
                                                   ST[:, r * 64:(r + 1) * 64],
                                                   PCE[:, r:r + 1], None, ALU.mult)
                        vsync(e, last)
                        e.wait_ge(s_pe, w)
                        e.tensor_tensor(ST[:, pg * 512:(pg + 1) * 512],
                                        ST[:, pg * 512:(pg + 1) * 512],
                                        ps[0:128, 0:512], ALU.add).then_inc(s_dv, 1)
                    emit("vector", sadd)
                    c.bump("s_dv")

            for b in range(NB):
                v_state_ready = c.val("s_dv")
                def sv_state(e, b=b, w=v_state_ready):
                    e.wait_ge(s_dv, w)
                    e.dma_start(out=sv_st.ap()[b * 128:(b + 1) * 128, :],
                                in_=ST[:]).then_inc(s_sg, 16)
                emit("gpsimd", sv_state)
                c.bump("s_sg", 16)
                v_stsave = c.val("s_sg")

                wait("vector", s_sg, c.val("s_sg"))
                for j in range(NT):
                    row0 = b * BL + j * 128
                    v_need_pe = c.val("s_pe")
                    v_need_dv = c.val("s_dv")
                    def xdma(e, row0=row0, wp=v_need_pe, wd=v_need_dv):
                        if wp > 0:
                            e.wait_ge(s_pe, wp)
                        if wd > 0:
                            e.wait_ge(s_dv, wd)
                        e.dma_start(out=XRB[:], in_=x_full[row0:row0 + 128, :]).then_inc(s_x, 16)
                    emit("gpsimd", xdma)
                    c.bump("s_x", 16)
                    v_x = c.val("s_x")

                    wait("scalar", s_x, v_x)
                    wait("scalar", s_dv, c.val("s_dv"))
                    def st_acc(e):
                        for q in range(4):
                            e.activation(SC1[:, 0:1024], XRB[:, q * 1024:(q + 1) * 1024],
                                         AF.Square, accum_out=STAT[:, 8 + q:9 + q])
                        e.activation(SC1[:, 0:1024], XRB[:, 0:1024], AFCOPY,
                                     accum_out=STAT[:, 12:13])
                        e.activation(SC1[:, 0:1024], XRB[:, 1024:2048], AFCOPY,
                                     accum_out=STAT[:, 13:14])
                        e.activation(SC1[:, 0:1024], XRB[:, 2048:3072], AFCOPY,
                                     accum_out=STAT[:, 14:15])
                        e.activation(SC1[:, 0:1024], XRB[:, 3072:4096], AFCOPY,
                                     accum_out=STAT[:, 15:16]).then_inc(s_ac, 1)
                    emit("scalar", st_acc)
                    c.bump("s_ac")
                    v_acc = c.val("s_ac")

                    wait("vector", s_x, v_x)
                    def st_red(e, w=v_acc):
                        e.reduce_max(STAT[:, 0:1], XRB[:], axis=AX)
                        e.tensor_reduce(STAT[:, 1:2], XRB[:], axis=AX, op=ALU.min)
                        e.wait_ge(s_ac, w)
                        vsync(e, e.reduce_sum(STAT[:, 2:3], STAT[:, 12:16], axis=AX))
                        vsync(e, e.reduce_sum(STAT[:, 3:4], STAT[:, 8:12], axis=AX))
                        vsync(e, e.tensor_scalar(STAT[:, 2:3], STAT[:, 2:3], 1.0 / H, None, ALU.mult))
                        vsync(e, e.tensor_scalar(STAT[:, 3:4], STAT[:, 3:4], 1.0 / H, None, ALU.mult))
                        vsync(e, e.tensor_tensor(STAT[:, 4:5], STAT[:, 2:3], STAT[:, 2:3], ALU.mult))
                        vsync(e, e.tensor_tensor(STAT[:, 4:5], STAT[:, 3:4], STAT[:, 4:5], ALU.subtract))
                        e.tensor_scalar(STAT[:, 10:11], STAT[:, 4:5], EPS, None,
                                        ALU.add).then_inc(s_dv, 1)
                    emit("vector", st_red)
                    c.bump("s_dv")
                    wait("scalar", s_dv, c.val("s_dv"))
                    def st_rstd(e):
                        asyncb(e, e.activation(STAT[:, 5:6], STAT[:, 10:11], AF.Ln))
                        e.activation(STAT[:, 6:7], STAT[:, 5:6], AF.Exp, scale=-0.5)
                        e.activation(STAT[:, 7:8], STAT[:, 5:6], AF.Exp,
                                     scale=-1.0).then_inc(s_ac, 1)
                    emit("scalar", st_rstd)
                    c.bump("s_ac")
                    wait("vector", s_ac, c.val("s_ac"))
                    def st_rms(e):
                        vsync(e, e.tensor_tensor(STAT[:, 7:8], STAT[:, 4:5], STAT[:, 7:8], ALU.mult))
                        e.tensor_scalar(STAT[:, 7:8], STAT[:, 7:8], EPS, None,
                                        ALU.add).then_inc(s_dv, 1)
                    emit("vector", st_rms)
                    c.bump("s_dv")
                    wait("scalar", s_dv, c.val("s_dv"))
                    def st_rstd2(e):
                        asyncb(e, e.activation(STAT[:, 7:8], STAT[:, 7:8], AF.Ln))
                        e.activation(STAT[:, 7:8], STAT[:, 7:8], AF.Exp,
                                     scale=-0.5).then_inc(s_ac, 1)
                    emit("scalar", st_rstd2)
                    c.bump("s_ac")
                    wait("vector", s_ac, c.val("s_ac"))
                    def st_apply(e, j=j):
                        vsync(e, e.tensor_tensor(STAT[:, 8:9], STAT[:, 6:7], STAT[:, 7:8], ALU.mult))
                        vsync(e, e.tensor_tensor(STAT[:, 9:10], STAT[:, 2:3], STAT[:, 8:9], ALU.mult))
                        vsync(e, e.tensor_scalar(STAT[:, 9:10], STAT[:, 9:10], -1.0, None, ALU.mult))
                        vsync(e, e.tensor_tensor(STAT[:, 0:1], STAT[:, 0:1], STAT[:, 2:3], ALU.subtract))
                        vsync(e, e.tensor_tensor(STAT[:, 1:2], STAT[:, 2:3], STAT[:, 1:2], ALU.subtract))
                        vsync(e, e.tensor_tensor(STAT[:, 0:1], STAT[:, 0:1], STAT[:, 1:2], ALU.max))
                        vsync(e, e.tensor_tensor(STAT[:, 0:1], STAT[:, 0:1], STAT[:, 6:7], ALU.mult))
                        e.tensor_tensor(AMX[:], AMX[:], STAT[:, 0:1], ALU.max)
                        e.tensor_copy(LNSV[:, 2 * j:2 * j + 1], STAT[:, 2:3])
                        e.tensor_copy(LNSV[:, 2 * j + 1:2 * j + 2], STAT[:, 6:7])
                        e.tensor_scalar(XRB[:], XRB[:], STAT[:, 8:9], STAT[:, 9:10],
                                        ALU.mult, ALU.add).then_inc(s_dv, 1)
                    emit("vector", st_apply)
                    c.bump("s_dv")
                    v_apply = c.val("s_dv")

                    wait("tensor", s_dv, v_apply)
                    if b > 0 and j == 0:
                        wait("tensor", s_sg, c.val("s_sg"))
                    for kc in range(KC):
                        ps = P6 if kc % 2 == 0 else P7
                        v_ev = c.val("s_ac") - 1
                        def tr(e, kc=kc, ps=ps, w=v_ev):
                            if w > 0:
                                e.wait_ge(s_ac, w)
                            e.transpose(ps[0:128, 0:128], XRB[:, kc * 128:(kc + 1) * 128],
                                        IDT[:]).then_inc(s_pe, 1)
                        emit("tensor", tr)
                        c.bump("s_pe")
                        wait("scalar", s_pe, c.val("s_pe"))
                        def trev(e, j=j, kc=kc, ps=ps):
                            e.activation(XRT[:, kc * BL + j * 128: kc * BL + j * 128 + 128],
                                         ps[0:128, 0:128], AFCOPY,
                                         scale=LNW[:, kc:kc + 1]).then_inc(s_ac, 1)
                        emit("scalar", trev)
                        c.bump("s_ac")
                    if b == 0 and j == 0:
                        checkpoint("tile1")
                    if b == 0 and j == 1:
                        checkpoint("tile2")

                checkpoint("stageA")
                def sv_lnsv(e, b=b, w=c.val("s_dv")):
                    e.wait_ge(s_dv, w)
                    e.dma_start(out=sv_lns.ap()[b * 128:(b + 1) * 128, :],
                                in_=LNSV[:]).then_inc(s_sg, 16)
                emit("gpsimd", sv_lnsv)
                c.bump("s_sg", 16)

                groups = [list(range(3 * g, 3 * g + 3)) for g in range(11)] + [[33, "dt"]]
                for g, chs in enumerate(groups):
                    col0 = H + 128 * chs[0]
                    ncols = sum(64 if ch == "dt" else 128 for ch in chs)
                    for ks in range(8):
                        src = w_in[ks * 512:(ks + 1) * 512, col0:col0 + ncols]
                        src = src.rearrange("(kc p) n -> p kc n", p=128)
                        WSL = WSA if ks % 2 == 0 else WSB
                        dstv = WSL[:, 0:4 * ncols].rearrange("p (kc n) -> p kc n", kc=4)
                        v_slabfree = c.val("s_pe") - (1 if ks >= 2 else 0)
                        def wd(e, src=src, dstv=dstv, w=v_slabfree):
                            if w > 0:
                                e.wait_ge(s_pe, w)
                            e.dma_start(out=dstv, in_=src).then_inc(s_w, 16)
                        emit("gpsimd", wd)
                        c.bump("s_w", 16)
                        wait("tensor", s_w, c.val("s_w"))
                        if ks == 0:
                            wait("tensor", s_ac, c.val("s_ac"))
                            wait("tensor", s_dv, c.val("s_dv"))
                        for kci in range(4):
                            kc = ks * 4 + kci
                            off = 0
                            for ci, ch in enumerate(chs):
                                w_off = kci * ncols + off
                                if ch == "dt":
                                    for hh in range(2):
                                        last = (kci == 3 and hh == 1 and ci == len(chs) - 1)
                                        def mmd(e, kc=kc, hh=hh, w_off=w_off, last=last, WSL=WSL):
                                            mm = e.matmul(
                                                P[6 + hh][0:64, 0:512],
                                                WSL[:, w_off:w_off + 64],
                                                XRT[:, kc * BL + hh * 512: kc * BL + hh * 512 + 512],
                                                start=(kc == 0), stop=(kc == 31))
                                            if last:
                                                mm.then_inc(s_pe, 1)
                                        emit("tensor", mmd)
                                    off += 64
                                else:
                                    pbase = 2 * ci
                                    for hh in range(2):
                                        last = (kci == 3 and ci == len(chs) - 1 and hh == 1)
                                        def mmx(e, kc=kc, hh=hh, w_off=w_off, pbase=pbase,
                                                last=last, WSL=WSL):
                                            mm = e.matmul(
                                                P[pbase + hh][0:128, 0:512],
                                                WSL[:, w_off:w_off + 128],
                                                XRT[:, kc * BL + hh * 512: kc * BL + hh * 512 + 512],
                                                start=(kc == 0), stop=(kc == 31))
                                            if last:
                                                mm.then_inc(s_pe, 1)
                                        emit("tensor", mmx)
                                    off += 128
                        c.bump("s_pe")
                    v_gdone = c.val("s_pe")

                    for ci, ch in enumerate(chs):
                        if ch == "dt":
                            wait("vector", s_pe, v_gdone)
                            def dtev(e):
                                e.tensor_scalar(SC1[0:64, 0:512], P6[0:64, 0:512], MUD[:, 0:1], None, ALU.mult)
                                e.tensor_scalar(SC1[0:64, 512:1024], P7[0:64, 0:512], MUD[:, 0:1], None, ALU.mult)
                                e.tensor_scalar(SC1[0:64, 0:1024], SC1[0:64, 0:1024], DTBS[:, 0:1],
                                                None, ALU.add).then_inc(s_dv, 1)
                            emit("vector", dtev)
                            c.bump("s_dv")
                            wait("scalar", s_dv, c.val("s_dv"))
                            def dtact(e):
                                e.activation(SC2[0:64, 0:1024], SC1[0:64, 0:1024], AF.Abs)
                                e.activation(SC2[0:64, 0:1024], SC2[0:64, 0:1024], AF.Exp,
                                             scale=-1.0).then_inc(s_ac, 1)
                            emit("scalar", dtact)
                            c.bump("s_ac")
                            wait("vector", s_ac, c.val("s_ac"))
                            def dtp1(e):
                                e.tensor_scalar(SC2[0:64, 0:1024], SC2[0:64, 0:1024], 1.0, None,
                                                ALU.add).then_inc(s_dv, 1)
                            emit("vector", dtp1)
                            c.bump("s_dv")
                            wait("scalar", s_dv, c.val("s_dv"))
                            def dtact2(e):
                                e.activation(SC2[0:64, 0:1024], SC2[0:64, 0:1024],
                                             AF.Ln).then_inc(s_ac, 1)
                            emit("scalar", dtact2)
                            c.bump("s_ac")
                            wait("vector", s_ac, c.val("s_ac"))
                            def dtfin(e):
                                e.tensor_scalar(SC1[0:64, 0:1024], SC1[0:64, 0:1024], 0.0, None, ALU.max)
                                e.tensor_add(DTF[:, :], SC1[0:64, 0:1024], SC2[0:64, 0:1024])
                                e.tensor_scalar(ADT[:, :], DTF[:, :], AV[:, 0:1], None, ALU.mult)
                                last = None
                                for cc in range(8):
                                    last = e.tensor_tensor_scan(
                                        ACD[:, cc * 128:(cc + 1) * 128],
                                        ADT[:, cc * 128:(cc + 1) * 128],
                                        ADT[:, cc * 128:(cc + 1) * 128],
                                        0.0, ALU.add, ALU.bypass)
                                last.then_inc(s_dv, 1)
                            emit("vector", dtfin)
                            c.bump("s_dv")
                            continue

                        pbase = 2 * ci
                        wait("vector", s_pe, v_gdone)
                        def xev(e, ch=ch, pbase=pbase):
                            e.tensor_scalar(SC2[:, 0:512], P[pbase][0:128, 0:512],
                                            MUP[:, ch:ch + 1], None, ALU.mult)
                            e.tensor_scalar(SC2[:, 512:1024], P[pbase + 1][0:128, 0:512],
                                            MUP[:, ch:ch + 1], None, ALU.mult)
                            e.tensor_scalar(SC1[:, 0:1024], SC2[:, 0:1024],
                                            CW[:, ch * 4 + 3:ch * 4 + 4], None, ALU.mult)
                            for k in range(3):
                                hn = 3 - k
                                e.scalar_tensor_tensor(SC1[:, 0:hn],
                                                       TAIL[:, ch * 3 + k:ch * 3 + 3],
                                                       CW[:, ch * 4 + k:ch * 4 + k + 1],
                                                       SC1[:, 0:hn], ALU.mult, ALU.add)
                                e.scalar_tensor_tensor(SC1[:, hn:1024],
                                                       SC2[:, 0:1024 - hn],
                                                       CW[:, ch * 4 + k:ch * 4 + k + 1],
                                                       SC1[:, hn:1024], ALU.mult, ALU.add)
                            e.tensor_copy(TAIL[:, ch * 3:ch * 3 + 3],
                                          SC2[:, 1021:1024]).then_inc(s_dv, 1)
                        emit("vector", xev)
                        c.bump("s_dv")
                        wait("scalar", s_dv, c.val("s_dv"))
                        def xsl(e):
                            e.activation(SC2[:, 0:1024], SC1[:, 0:1024], AF.Silu).then_inc(s_ac, 1)
                        emit("scalar", xsl)
                        c.bump("s_ac")
                        v_silu = c.val("s_ac")

                        if ch < 32:
                            wait("tensor", s_ac, v_silu)
                            for u in range(8):
                                ps = P6 if u % 2 == 0 else P7
                                v_ev = c.val("s_dv") - 1
                                def trx(e, u=u, ps=ps, w=v_ev):
                                    if w > 0:
                                        e.wait_ge(s_dv, w)
                                    e.transpose(ps[0:128, 0:128], SC2[:, u * 128:(u + 1) * 128],
                                                IDT[:]).then_inc(s_pe, 1)
                                emit("tensor", trx)
                                c.bump("s_pe")
                                wait("vector", s_pe, c.val("s_pe"))
                                def trxe(e, u=u, ch=ch, ps=ps):
                                    e.tensor_copy(XS[:, u * H + ch * 128: u * H + ch * 128 + 128],
                                                  ps[0:128, 0:128]).then_inc(s_dv, 1)
                                emit("vector", trxe)
                                c.bump("s_dv")
                        elif ch == 32:
                            wait("vector", s_ac, v_silu)
                            def bcp(e):
                                e.tensor_copy(BF[:, :], SC2[:, 0:1024]).then_inc(s_dv, 1)
                            emit("vector", bcp)
                            c.bump("s_dv")
                            wait("tensor", s_ac, v_silu)
                            for u in range(8):
                                ps = P6 if u % 2 == 0 else P7
                                v_ev = c.val("s_dv") - 1
                                def trb(e, u=u, ps=ps, w=v_ev):
                                    if w > 0:
                                        e.wait_ge(s_dv, w)
                                    e.transpose(ps[0:128, 0:128], SC2[:, u * 128:(u + 1) * 128],
                                                IDT[:]).then_inc(s_pe, 1)
                                emit("tensor", trb)
                                c.bump("s_pe")
                                wait("vector", s_pe, c.val("s_pe"))
                                def trbe(e, u=u, ps=ps):
                                    e.tensor_copy(BT8[:, u * 128:(u + 1) * 128],
                                                  ps[0:128, 0:128]).then_inc(s_dv, 1)
                                emit("vector", trbe)
                                c.bump("s_dv")
                        else:
                            wait("vector", s_ac, v_silu)
                            def ccp(e):
                                e.tensor_copy(CF[:, :], SC2[:, 0:1024]).then_inc(s_dv, 1)
                            emit("vector", ccp)
                            c.bump("s_dv")

                checkpoint("stageB")
                for u in range(16):
                    r0 = b * 512 + (u // 4) * 128
                    c0w = (u % 4) * 1024
                    v_prevred = c.val("s_dv")
                    def wfx(e, r0=r0, c0w=c0w, w=v_prevred):
                        e.wait_ge(s_dv, w)
                        e.dma_start(out=SC1[:, 0:1024],
                                    in_=w_ffn[r0:r0 + 128, c0w:c0w + 1024]).then_inc(s_x, 16)
                    emit("gpsimd", wfx)
                    c.bump("s_x", 16)
                    wait("vector", s_x, c.val("s_x"))
                    def wfr(e):
                        vsync(e, e.reduce_max(STAT[:, 0:1], SC1[:, 0:1024], axis=AX,
                                              apply_absolute_value=True))
                        e.tensor_tensor(WMX[:], WMX[:], STAT[:, 0:1], ALU.max).then_inc(s_dv, 1)
                    emit("vector", wfr)
                    c.bump("s_dv")

                def sv_big(e, b=b, w=c.val("s_dv"), wa=c.val("s_ac")):
                    e.wait_ge(s_dv, w)
                    e.wait_ge(s_ac, wa)
                    r0, r1 = b * 128, (b + 1) * 128
                    e.dma_start(out=sv_xrt.ap()[r0:r1, :], in_=XRT[:]).then_inc(s_sg, 16)
                    e.dma_start(out=sv_xs.ap()[r0:r1, :], in_=XS[:]).then_inc(s_sg, 16)
                    e.dma_start(out=sv_bc.ap()[r0:r1, 0:BL], in_=BF[:]).then_inc(s_sg, 16)
                    e.dma_start(out=sv_bc.ap()[r0:r1, BL:2 * BL], in_=CF[:]).then_inc(s_sg, 16)
                    e.dma_start(out=sv_bc.ap()[r0:r1, 2 * BL:3 * BL], in_=BT8[:]).then_inc(s_sg, 16)
                    e.dma_start(out=sv_dt.ap()[b * 64:(b + 1) * 64, :],
                                in_=DTF[:, :]).then_inc(s_sg, 16)
                emit("gpsimd", sv_big)
                c.bump("s_sg", 6 * 16)

                for cc in range(8):
                    chunk_tokens(cc, with_ldt=False)
                    state_update(cc, extra_wait_sv=(v_stsave if cc == 0 else 0))
                checkpoint("stageD")

            checkpoint("phase1")
            v1d, v1a, v1p, v1s = c.val("s_dv"), c.val("s_ac"), c.val("s_pe"), c.val("s_sg")

            def rld(e, wd=v1d, wa=v1a, wp=v1p, ws=v1s):
                e.wait_ge(s_dv, wd)
                e.wait_ge(s_ac, wa)
                e.wait_ge(s_pe, wp)
                e.wait_ge(s_sg, ws)
                with e.register("myo0") as fr:
                    e.reg_load(fr, MYO[0:1, 0:1])
                    v0 = e.snap(fr, min_val=0, max_val=(NB - 1) * 128)
                    e.dma_start(out=XRT[:], in_=sv_xrt.ap()[bass.ds(v0, 128), :]).then_inc(s_sg, 16)
                    e.dma_start(out=XS[:], in_=sv_xs.ap()[bass.ds(v0, 128), :]).then_inc(s_sg, 16)
                    e.dma_start(out=BF[:], in_=sv_bc.ap()[bass.ds(v0, 128), 0:BL]).then_inc(s_sg, 16)
                    e.dma_start(out=CF[:], in_=sv_bc.ap()[bass.ds(v0, 128), BL:2 * BL]).then_inc(s_sg, 16)
                    e.dma_start(out=BT8[:], in_=sv_bc.ap()[bass.ds(v0, 128), 2 * BL:3 * BL]).then_inc(s_sg, 16)
                    e.dma_start(out=ST[:], in_=sv_st.ap()[bass.ds(v0, 128), :]).then_inc(s_sg, 16)
                    e.dma_start(out=LNSV[:], in_=sv_lns.ap()[bass.ds(v0, 128), :]).then_inc(s_sg, 16)
                with e.register("myo1") as fr:
                    e.reg_load(fr, MYO[0:1, 1:2])
                    v1 = e.snap(fr, min_val=0, max_val=(NB - 1) * 64)
                    e.dma_start(out=DTF[:, :], in_=sv_dt.ap()[bass.ds(v1, 64), :]).then_inc(s_sg, 16)
            emit("gpsimd", rld)
            c.bump("s_sg", 8 * 16)
            v_rld = c.val("s_sg")

            wait("vector", s_sg, v_rld)
            def p2dt(e):
                e.tensor_scalar(ADT[:, :], DTF[:, :], AV[:, 0:1], None, ALU.mult)
                last = None
                for cc in range(8):
                    last = e.tensor_tensor_scan(ACD[:, cc * 128:(cc + 1) * 128],
                                                ADT[:, cc * 128:(cc + 1) * 128],
                                                ADT[:, cc * 128:(cc + 1) * 128],
                                                0.0, ALU.add, ALU.bypass)
                last.then_inc(s_dv, 1)
            emit("vector", p2dt)
            c.bump("s_dv")

            wait("tensor", s_sg, v_rld)
            wait("tensor", s_dv, c.val("s_dv"))
            def amt(e):
                e.transpose(P6[0:1, 0:128], AMX[:, 0:1], IDT[:])
                e.transpose(P6[0:1, 128:256], WMX[:, 0:1], IDT[:]).then_inc(s_pe, 1)
            emit("tensor", amt)
            c.bump("s_pe")
            wait("vector", s_pe, c.val("s_pe"))
            def amr(e):
                vsync(e, e.tensor_copy(SM1[0:1, 8:136], P6[0:1, 0:128]))
                vsync(e, e.reduce_max(SM1[0:1, 0:1], SM1[0:1, 8:136], axis=AX))
                vsync(e, e.tensor_copy(SM1[0:1, 8:136], P6[0:1, 128:256]))
                vsync(e, e.reduce_max(SM1[0:1, 1:2], SM1[0:1, 8:136], axis=AX))
                vsync(e, e.tensor_scalar(SM1[0:1, 0:2], SM1[0:1, 0:2], 1e-12, None, ALU.max))
                vsync(e, e.tensor_tensor(SM1[0:1, 2:3], SM1[0:1, 0:1], SM1[0:1, 1:2], ALU.mult))
                vsync(e, e.tensor_scalar(SM1[0:1, 2:3], SM1[0:1, 2:3], 1.0 / (FP8M * FP8M), None, ALU.mult))
                vsync(e, e.reciprocal(SM1[0:1, 3:4], SM1[0:1, 0:1]))
                vsync(e, e.tensor_scalar(SM1[0:1, 4:5], SM1[0:1, 0:1], 1.0 / FP8M, None, ALU.mult))
                e.tensor_scalar(SM1[0:1, 3:4], SM1[0:1, 3:4], FP8M, None, ALU.mult).then_inc(s_dv, 1)
            emit("vector", amr)
            c.bump("s_dv")
            wait("tensor", s_dv, c.val("s_dv"))
            def amb(e):
                e.matmul(P6[0:128, 0:3], ON1[0:1, 0:128], SM1[0:1, 2:5],
                         start=True, stop=True).then_inc(s_pe, 1)
            emit("tensor", amb)
            c.bump("s_pe")
            wait("scalar", s_pe, c.val("s_pe"))
            def ambe(e):
                e.copy(INVB[:], P6[0:128, 0:1])
                e.copy(SXB[:], P6[0:128, 1:2])
                e.copy(SWBC[:], P6[0:128, 2:3]).then_inc(s_ac, 1)
            emit("scalar", ambe)
            c.bump("s_ac")
            v_scales = c.val("s_ac")

            checkpoint("rld")
            for cc in range(8):
                chunk_tokens(cc, with_ldt=True)
                if cc == 0:
                    checkpoint("p2t1")
                wait("vector", s_ac, c.val("s_ac"))
                def t2b(e):
                    vsync(e, e.tensor_tensor(BIAS[:], BIAS[:], ACDT[:], ALU.subtract))
                    e.tensor_copy(SBF[:], ST[:]).then_inc(s_dv, 1)
                emit("vector", t2b)
                c.bump("s_dv")
                if cc == 0:
                    checkpoint("p2t2")

                if cc < 7:
                    state_update(cc)
                if cc == 0:
                    checkpoint("p2a")

                def acw(e, cc=cc, w=c.val("s_ac"), wsv=c.val("s_sg")):
                    e.wait_ge(s_ac, w)
                    e.wait_ge(s_sg, wsv)
                    e.dma_start(out=acd_dram.ap(),
                                in_=ACD[:, cc * 128:(cc + 1) * 128]).then_inc(s_sg, 16)
                emit("gpsimd", acw)
                c.bump("s_sg", 16)
                v_acw = c.val("s_sg")

                def gtm(e, cc=cc, w=c.val("s_dv")):
                    e.wait_ge(s_dv, w)
                    e.matmul(P2[0:128, 0:128], BF[:, cc * 128:(cc + 1) * 128],
                             CF[:, cc * 128:(cc + 1) * 128], start=True, stop=True).then_inc(s_pe, 1)
                emit("tensor", gtm)
                c.bump("s_pe")
                wait("vector", s_pe, c.val("s_pe"))
                def gtc(e):
                    vsync(e, e.tensor_tensor(GTC, P2[0:128, 0:128], CAUS[:], ALU.mult))
                    e.nop().then_inc(s_dv, 1)
                emit("vector", gtc)
                c.bump("s_dv")
                v_gtc = c.val("s_dv")
                if cc == 0:
                    checkpoint("p2b")

                v_state_done = c.val("s_dv")
                for pg in range(8):
                    pa = P0 if pg % 2 == 0 else P1
                    pb = P4 if pg % 2 == 0 else P5
                    v_acb_half = [0, 0]
                    for hb in range(2):
                        v_prev_msk = c.val("s_dv")
                        def acbd(e, pg=pg, hb=hb, w=v_acw, wm=v_prev_msk):
                            e.wait_ge(s_sg, w)
                            e.wait_ge(s_dv, wm)
                            e.dma_start(out=ACB[:],
                                        in_=acd_dram.ap().rearrange("r t -> (r t)")
                                        [None, pg * 1024 + hb * 512:pg * 1024 + (hb + 1) * 512]
                                        .partition_broadcast(128)).then_inc(s_sg, 16)
                        if hb == 0:
                            emit("gpsimd", acbd)
                            c.bump("s_sg", 16)
                            v_acb_half[0] = c.val("s_sg")
                        else:
                            acbd_late = acbd
                    for ri in range(8):
                        r = pg * 8 + ri
                        if ri == 4:
                            v_prev_msk2 = c.val("s_dv")
                            def acbd2(e, pg=pg, w=v_acw, wm=v_prev_msk2):
                                e.wait_ge(s_sg, w)
                                e.wait_ge(s_dv, wm)
                                e.dma_start(out=ACB[:],
                                            in_=acd_dram.ap().rearrange("r t -> (r t)")
                                            [None, pg * 1024 + 512:pg * 1024 + 1024]
                                            .partition_broadcast(128)).then_inc(s_sg, 16)
                            emit("gpsimd", acbd2)
                            c.bump("s_sg", 16)
                            v_acb_half[1] = c.val("s_sg")
                        v_imm_prev = c.val("s_pe")
                        v_evac_prev = c.val("s_dv")
                        def msk(e, r=r, ri=ri, pg=pg, wi=v_imm_prev, wev=v_evac_prev,
                                wg=v_gtc, wb=v_acb_half[ri // 4]):
                            if ri in (0, 4):
                                e.wait_ge(s_sg, wb)
                            if ri == 0:
                                e.wait_ge(s_dv, wg)
                            if wi > 0:
                                e.wait_ge(s_pe, wi)
                            if ri == 0 and pg >= 2 and wev > 0:
                                e.wait_ge(s_dv, wev)
                            e.tensor_scalar(MSK, ACB[:, (ri % 4) * 128:(ri % 4 + 1) * 128],
                                            BIAS[:, r:r + 1], 88.0, ALU.add, ALU.min).then_inc(s_dv, 1)
                        emit("vector", msk)
                        c.bump("s_dv")
                        wait("scalar", s_dv, c.val("s_dv"))
                        emit("scalar", lambda e: e.activation(MSK, MSK, AF.Exp).then_inc(s_ac, 1))
                        c.bump("s_ac")
                        wait("vector", s_ac, c.val("s_ac"))
                        def mskg(e):
                            e.tensor_tensor(GDB[:], MSK, GTC, ALU.mult).then_inc(s_dv, 1)
                        emit("vector", mskg)
                        c.bump("s_dv")
                        v_gd = c.val("s_dv")
                        def imm(e, r=r, ri=ri, cc=cc, pa=pa, w=v_gd):
                            e.wait_ge(s_dv, w)
                            e.matmul(pa[0:128, ri * 64:(ri + 1) * 64], GDB[:],
                                     XS[:, cc * H + r * 64:cc * H + r * 64 + 64],
                                     start=(ri == 0), stop=False).then_inc(s_pe, 1)
                        emit("tensor", imm)
                        c.bump("s_pe")

                    if cc == 0 and pg == 0:
                        checkpoint("p2c")
                    v_prev_mm = c.val("s_pe")
                    def xdpd(e, cc=cc, pg=pg, w=c.val("s_dv"), wp=v_prev_mm):
                        e.wait_ge(s_dv, w)
                        e.wait_ge(s_pe, wp)
                        for ri in range(8):
                            r = pg * 8 + ri
                            last = e.tensor_scalar(XDP[:, ri * 64:(ri + 1) * 64],
                                                   XS[:, cc * H + r * 64:cc * H + r * 64 + 64],
                                                   DTOK[:, r:r + 1], None, ALU.mult)
                        vsync(e, last)
                        e.nop().then_inc(s_dv, 1)
                    emit("vector", xdpd)
                    c.bump("s_dv")
                    wait("tensor", s_dv, c.val("s_dv"))
                    def ifo(e, pa=pa):
                        e.matmul(pa[0:128, 0:512], IDB[:], XDP[:, 0:512],
                                 start=False, stop=True).then_inc(s_pe, 1)
                    emit("tensor", ifo)
                    c.bump("s_pe")
                    def yst(e, cc=cc, pg=pg, pb=pb):
                        e.matmul(pb[0:128, 0:512], CF[:, cc * 128:(cc + 1) * 128],
                                 SBF[:, pg * 512:(pg + 1) * 512],
                                 start=True, stop=True).then_inc(s_pe, 1)
                    emit("tensor", yst)
                    c.bump("s_pe")
                    v_ymm = c.val("s_pe")
                    def yev(e, cc=cc, pg=pg, pa=pa, pb=pb, w=v_ymm):
                        e.wait_ge(s_pe, w)
                        for ri in range(8):
                            r = pg * 8 + ri
                            last = e.tensor_scalar(YEV[:, ri * 64:(ri + 1) * 64],
                                                   pb[0:128, ri * 64:(ri + 1) * 64],
                                                   EACD[:, r:r + 1], None, ALU.mult)
                        vsync(e, last)
                        vsync(e, e.tensor_tensor(YEV, YEV, pa[0:128, 0:512], ALU.add))
                        e.tensor_copy(XS[:, cc * H + pg * 512:cc * H + (pg + 1) * 512],
                                      YEV).then_inc(s_dv, 1)
                    emit("vector", yev)
                    c.bump("s_dv")
                    if cc == 0 and pg == 0:
                        checkpoint("p2d")
                if cc == 0:
                    checkpoint("p2e")

            if debug:
                def dbg1(e, w=c.val("s_dv"), wa=c.val("s_ac")):
                    e.wait_ge(s_dv, w)
                    e.wait_ge(s_ac, wa)
                    e.dma_start(out=dbg["d_state"], in_=ST[:]).then_inc(s_st, 16)
                    e.dma_start(out=dbg["d_dt"], in_=DTQ2[:]).then_inc(s_st, 16)
                    e.dma_start(out=dbg["d_dtq"], in_=DTQ[:]).then_inc(s_st, 16)
                emit("gpsimd", dbg1)
                c.bump("s_st", 48)
                def dbgbc(e, w=c.val("s_dv")):
                    e.wait_ge(s_dv, w)
                    e.tensor_copy(SC1[:, 0:1024], BF[:, :])
                    e.tensor_copy(SC2[:, 0:1024], CF[:, :]).then_inc(s_dv, 1)
                emit("vector", dbgbc)
                c.bump("s_dv")
                def dbgbc2(e, w=c.val("s_dv")):
                    e.wait_ge(s_dv, w)
                    e.dma_start(out=dbg["d_misc"][:, 0:1024], in_=SC1[:, 0:1024]).then_inc(s_st, 16)
                    e.dma_start(out=dbg["d_misc"][:, 1024:2048], in_=SC2[:, 0:1024]).then_inc(s_st, 16)
                emit("gpsimd", dbgbc2)
                c.bump("s_st", 32)
                wait("vector", s_st, c.val("s_st"))
                for u in range(NT):
                    def dbgy(e, u=u, w=c.val("s_dv")):
                        e.wait_ge(s_dv, w)
                        e.tensor_copy(XRB[:], XS[:, u * H:(u + 1) * H]).then_inc(s_dv, 1)
                    emit("vector", dbgy)
                    c.bump("s_dv")
                    def dbgy2(e, u=u, w=c.val("s_dv")):
                        e.wait_ge(s_dv, w)
                        e.dma_start(out=dbg["d_y"][:, u * H:(u + 1) * H],
                                    in_=XRB[:]).then_inc(s_st, 16)
                    emit("gpsimd", dbgy2)
                    c.bump("s_st", 16)
                    wait("vector", s_st, c.val("s_st"))

            checkpoint("p2scan")
            for nb in range(8):
                col0 = nb * 512
                def mzl(e, nb=nb, col0=col0, w=c.val("s_dv")):
                    e.wait_ge(s_dv, w)
                    e.dma_start(out=SC2[0:1, 512:1024],
                                in_=mup_z[0:1, col0:col0 + 512]).then_inc(s_sg, 16)
                emit("gpsimd", mzl)
                c.bump("s_sg", 16)
                wait("tensor", s_sg, c.val("s_sg"))
                wait("tensor", s_dv, c.val("s_dv"))
                wait("tensor", s_ac, c.val("s_ac"))
                def mzb(e):
                    e.matmul(P2[0:128, 0:512], ON1[0:1, 0:128], SC2[0:1, 512:1024],
                             start=True, stop=True).then_inc(s_pe, 1)
                emit("tensor", mzb)
                c.bump("s_pe")
                wait("scalar", s_pe, c.val("s_pe"))
                def mzbe(e):
                    e.copy(MZS, P2[0:128, 0:512]).then_inc(s_ac, 1)
                emit("scalar", mzbe)
                c.bump("s_ac")
                v_mzs = c.val("s_ac")

                for ks in range(8):
                    src = w_in[ks * 512:(ks + 1) * 512, col0:col0 + 512]
                    src = src.rearrange("(kc p) n -> p kc n", p=128)
                    WSL = WSA if ks % 2 == 0 else WSB
                    dstv = WSL[:, 0:4 * 512].rearrange("p (kc n) -> p kc n", kc=4)
                    v_slabfree = c.val("s_pe") - (1 if ks >= 2 else 0)
                    def wdz(e, src=src, dstv=dstv, w=v_slabfree):
                        if w > 0:
                            e.wait_ge(s_pe, w)
                        e.dma_start(out=dstv, in_=src).then_inc(s_w, 16)
                    emit("gpsimd", wdz)
                    c.bump("s_w", 16)
                    wait("tensor", s_w, c.val("s_w"))
                    if ks == 0:
                        wait("tensor", s_dv, c.val("s_dv"))
                    for kci in range(4):
                        kc = ks * 4 + kci
                        for j in range(8):
                            last = (kci == 3) and (j == 7)
                            def zmm(e, kc=kc, kci=kci, j=j, last=last, WSL=WSL):
                                mm = e.matmul(P[j][0:128, 0:512],
                                              XRT[:, kc * BL + j * 128:kc * BL + j * 128 + 128],
                                              WSL[:, kci * 512:(kci + 1) * 512],
                                              start=(kc == 0), stop=(kc == 31))
                                if last:
                                    mm.then_inc(s_pe, 1)
                            emit("tensor", zmm)
                    c.bump("s_pe")
                v_z = c.val("s_pe")
                for j in range(8):
                    wait("vector", s_pe, v_z)
                    wait("vector", s_ac, v_mzs)
                    def zev(e, j=j):
                        e.tensor_tensor(SC1[:, 0:512], P[j][0:128, 0:512],
                                        MZS, ALU.mult).then_inc(s_dv, 1)
                    emit("vector", zev)
                    c.bump("s_dv")
                    wait("scalar", s_dv, c.val("s_dv"))
                    def zsl(e):
                        e.activation(SC1[:, 512:1024], SC1[:, 0:512], AF.Silu).then_inc(s_ac, 1)
                    emit("scalar", zsl)
                    c.bump("s_ac")
                    wait("vector", s_ac, c.val("s_ac"))
                    def zg(e, j=j, nb=nb):
                        e.tensor_tensor(XS[:, j * H + nb * 512:j * H + (nb + 1) * 512],
                                        XS[:, j * H + nb * 512:j * H + (nb + 1) * 512],
                                        SC1[:, 512:1024], ALU.mult).then_inc(s_dv, 1)
                    emit("vector", zg)
                    c.bump("s_dv")

            checkpoint("zgate")
            for j in range(NT):
                wait("scalar", s_dv, c.val("s_dv"))
                def ysq(e, j=j):
                    for q in range(3):
                        e.activation(SC1[:, 0:1024], XS[:, j * H + q * 1024:j * H + (q + 1) * 1024],
                                     AF.Square, accum_out=STAT[:, 8 + q:9 + q])
                    e.activation(SC1[:, 0:1024], XS[:, j * H + 3 * 1024:j * H + 4 * 1024],
                                 AF.Square, accum_out=STAT[:, 11:12]).then_inc(s_ac, 1)
                emit("scalar", ysq)
                c.bump("s_ac")
                wait("vector", s_ac, c.val("s_ac"))
                def ysum(e):
                    vsync(e, e.reduce_sum(STAT[:, 0:1], STAT[:, 8:12], axis=AX))
                    vsync(e, e.tensor_scalar(STAT[:, 0:1], STAT[:, 0:1], 1.0 / H, None, ALU.mult))
                    e.tensor_scalar(STAT[:, 0:1], STAT[:, 0:1], EPS, None,
                                    ALU.add).then_inc(s_dv, 1)
                emit("vector", ysum)
                c.bump("s_dv")
                wait("scalar", s_dv, c.val("s_dv"))
                def yrs(e):
                    asyncb(e, e.activation(STAT[:, 1:2], STAT[:, 0:1], AF.Ln))
                    e.activation(STAT[:, 2:3], STAT[:, 1:2], AF.Exp, scale=-0.5).then_inc(s_ac, 1)
                emit("scalar", yrs)
                c.bump("s_ac")
                wait("vector", s_ac, c.val("s_ac"))
                wait("vector", s_pe, c.val("s_pe"))
                def yap(e, j=j):
                    e.tensor_scalar(XRB[:], XS[:, j * H:(j + 1) * H], STAT[:, 2:3],
                                    None, ALU.mult).then_inc(s_dv, 1)
                emit("vector", yap)
                c.bump("s_dv")
                v_yn = c.val("s_dv")
                wait("tensor", s_dv, v_yn)
                for kc in range(KC):
                    ps = P6 if kc % 2 == 0 else P7
                    v_ev = c.val("s_ac") - 1
                    def ytr(e, kc=kc, ps=ps, w=v_ev):
                        if w > 0:
                            e.wait_ge(s_ac, w)
                        e.transpose(ps[0:128, 0:128], XRB[:, kc * 128:(kc + 1) * 128],
                                    IDT[:]).then_inc(s_pe, 1)
                    emit("tensor", ytr)
                    c.bump("s_pe")
                    wait("scalar", s_pe, c.val("s_pe"))
                    def ytre(e, j=j, kc=kc, ps=ps):
                        e.activation(XRT[:, kc * BL + j * 128:kc * BL + j * 128 + 128],
                                     ps[0:128, 0:128], AFCOPY,
                                     scale=NRW[:, kc:kc + 1]).then_inc(s_ac, 1)
                    emit("scalar", ytre)
                    c.bump("s_ac")

            checkpoint("rms2")
            for j in range(NT):
                def xod(e, j=j, w=c.val("s_dv"), wp=c.val("s_pe")):
                    e.wait_ge(s_dv, w)
                    e.wait_ge(s_pe, wp)
                    e.dma_start(out=XRB[:], in_=x_own[j * 128:(j + 1) * 128, :]).then_inc(s_x, 16)
                emit("gpsimd", xod)
                c.bump("s_x", 16)
                wait("vector", s_x, c.val("s_x"))
                wait("vector", s_ac, v_scales)
                def xoa(e, j=j):
                    vsync(e, e.tensor_tensor(STAT[:, 0:1], LNSV[:, 2 * j + 1:2 * j + 2], SXB[:], ALU.mult))
                    vsync(e, e.tensor_tensor(STAT[:, 1:2], LNSV[:, 2 * j:2 * j + 1], STAT[:, 0:1], ALU.mult))
                    vsync(e, e.tensor_scalar(STAT[:, 1:2], STAT[:, 1:2], -1.0, None, ALU.mult))
                    e.tensor_scalar(XRB[:], XRB[:], STAT[:, 0:1], STAT[:, 1:2],
                                    ALU.mult, ALU.add).then_inc(s_dv, 1)
                emit("vector", xoa)
                c.bump("s_dv")
                v_xn = c.val("s_dv")
                wait("tensor", s_dv, v_xn)
                for kc in range(KC):
                    ps = P6 if kc % 2 == 0 else P7
                    v_ev = c.val("s_dv") - 1
                    def xtr(e, kc=kc, ps=ps, w=v_ev):
                        if w > 0:
                            e.wait_ge(s_dv, w)
                        e.transpose(ps[0:128, 0:128], XRB[:, kc * 128:(kc + 1) * 128],
                                    IDT[:]).then_inc(s_pe, 1)
                    emit("tensor", xtr)
                    c.bump("s_pe")
                    wait("vector", s_pe, c.val("s_pe"))
                    def xtre(e, j=j, kc=kc, ps=ps):
                        e.tensor_copy(XS[:, kc * BL + j * 128: kc * BL + j * 128 + 128],
                                      ps[0:128, 0:128]).then_inc(s_dv, 1)
                    emit("vector", xtre)
                    c.bump("s_dv")

            checkpoint("xnt")
            for nb in (range(7, -1, -1) if OUTREV else range(8)):
                col0 = nb * 512
                for half in range(2):
                    jset = [half * 4 + t for t in range(4)]
                    for ks in range(16):
                        srcO = w_out[ks * 256:(ks + 1) * 256, col0:col0 + 512] \
                            .rearrange("(kc p) n -> p kc n", p=128)
                        srcF = w_ffn[ks * 256:(ks + 1) * 256, col0:col0 + 512] \
                            .rearrange("(kc p) n -> p kc n", p=128)
                        WSL = WSA if ks % 2 == 0 else WSB
                        dstO = WSL[:, 0:2 * 512].rearrange("p (kc n) -> p kc n", kc=2)
                        dstF = WSL[:, 2 * 512:4 * 512].rearrange("p (kc n) -> p kc n", kc=2)
                        v_slabfree = c.val("s_pe") - (1 if ks >= 2 else 0)
                        v_stall = c.val("s_st") if OUTSER else 0
                        def wdo(e, srcO=srcO, srcF=srcF, dstO=dstO, dstF=dstF,
                                w=v_slabfree, wst=v_stall):
                            if wst > 0:
                                e.wait_ge(s_st, wst)
                            if w > 0:
                                e.wait_ge(s_pe, w)
                            e.dma_start(out=dstO, in_=srcO).then_inc(s_w, 16)
                            e.dma_start(out=dstF, in_=srcF).then_inc(s_w, 16)
                        emit("gpsimd", wdo)
                        c.bump("s_w", 32)
                        wait("tensor", s_w, c.val("s_w"))
                        if ks == 0:
                            wait("tensor", s_dv, c.val("s_dv"))
                            wait("tensor", s_ac, c.val("s_ac"))
                        for kci in range(2):
                            kc = ks * 2 + kci
                            for t4, j in enumerate(jset):
                                def omm(e, kc=kc, kci=kci, j=j, t4=t4, WSL=WSL):
                                    e.matmul(P[t4][0:128, 0:512],
                                             XRT[:, kc * BL + j * 128:kc * BL + j * 128 + 128],
                                             WSL[:, kci * 512:(kci + 1) * 512],
                                             start=(kc == 0), stop=(kc == 31))
                                emit("tensor", omm)
                            for t4, j in enumerate(jset):
                                last = (kci == 1) and (t4 == 3)
                                def fmm(e, kc=kc, kci=kci, j=j, t4=t4, WSL=WSL, last=last):
                                    mm = e.matmul(P[4 + t4][0:128, 0:512],
                                                  XS[:, kc * BL + j * 128:kc * BL + j * 128 + 128],
                                                  WSL[:, (2 + kci) * 512:(3 + kci) * 512],
                                                  start=(kc == 0), stop=(kc == 31))
                                    if last:
                                        mm.then_inc(s_pe, 1)
                                emit("tensor", fmm)
                        c.bump("s_pe")
                        if OUTSER:
                            wait("vector", s_pe, c.val("s_pe"))
                            wait("scalar", s_pe, c.val("s_pe"))
                    v_of = c.val("s_pe")
                    if OUTPS == (nb, half) and debug:
                        for k in range(8):
                            wait("scalar", s_pe, v_of)
                            def pcp(e, k=k):
                                e.copy(SC1[:, 0:512], P[k][0:128, 0:512]).then_inc(s_ac, 1)
                            emit("scalar", pcp)
                            c.bump("s_ac")
                            def pdm(e, k=k, w=c.val("s_ac")):
                                e.wait_ge(s_ac, w)
                                e.dma_start(out=dbg["d_y"][:, k * 512:(k + 1) * 512],
                                            in_=SC1[:, 0:512]).then_inc(s_st, 16)
                            emit("gpsimd", pdm)
                            c.bump("s_st", 16)
                            wait("scalar", s_st, c.val("s_st"))
                        wait("vector", s_st, c.val("s_st"))
                    for t4, j in enumerate(jset):
                        def xsl2(e, j=j, col0=col0, w=c.val("s_dv")):
                            e.wait_ge(s_dv, w)
                            e.dma_start(out=XOT,
                                        in_=x_own[j * 128:(j + 1) * 128, col0:col0 + 512]).then_inc(s_x, 16)
                        emit("gpsimd", xsl2)
                        c.bump("s_x", 16)
                        v_x3 = c.val("s_x")
                        sc_off = 0 if t4 % 2 == 0 else 512
                        v_store_prev = c.val("s_st") - 16
                        wait("vector", s_pe, v_of)
                        wait("vector", s_x, v_x3)
                        def fev(e, t4=t4, sc_off=sc_off, wst=v_store_prev):
                            if wst > 0:
                                e.wait_ge(s_st, wst)
                            e.tensor_scalar(SC2[:, sc_off:sc_off + 512], P[4 + t4][0:128, 0:512],
                                            SWBC[:, 0:1], None, ALU.mult)
                            e.tensor_tensor(SC2[:, sc_off:sc_off + 512], SC2[:, sc_off:sc_off + 512],
                                            P[t4][0:128, 0:512], ALU.add)
                            e.tensor_tensor(SC2[:, sc_off:sc_off + 512], SC2[:, sc_off:sc_off + 512],
                                            XOT, ALU.add).then_inc(s_dv, 1)
                        emit("vector", fev)
                        c.bump("s_dv")
                        v_fin = c.val("s_dv")
                        def ost(e, j=j, col0=col0, sc_off=sc_off, w=v_fin):
                            e.wait_ge(s_dv, w)
                            e.dma_start(out=out[j * 128:(j + 1) * 128, col0:col0 + 512],
                                        in_=SC2[:, sc_off:sc_off + 512]).then_inc(s_st, 16)
                        emit("gpsimd", ost)
                        c.bump("s_st", 16)
                        if OUTSER:
                            wait("vector", s_st, c.val("s_st"))
                            wait("tensor", s_st, c.val("s_st"))

            emit("gpsimd", lambda e, w=c.val("s_st"): e.wait_ge(s_st, w))

        except _CutDone:
            pass

        @block.sync
        def _(sync):
            for fn in prog["sync"]:
                fn(sync)

        @block.gpsimd
        def _(gpsimd):
            for fn in prog["gpsimd"]:
                fn(gpsimd)

        @block.tensor
        def _(tensor):
            for fn in prog["tensor"]:
                fn(tensor)

        @block.vector
        def _(vector):
            for fn in prog["vector"]:
                fn(vector)

        @block.scalar
        def _(scalar):
            for fn in prog["scalar"]:
                fn(scalar)

    return nc


def make_inputs(full):
    x = np.ascontiguousarray(np.asarray(full["L0_attn_hidden_states"], np.float32)[0])
    mup = np.asarray(full["L0_attn_mup_vector"], np.float32)
    common = dict(
        x_full=x,
        w_in=np.ascontiguousarray(np.asarray(full["L0_attn_in_proj_w"], np.float32)),
        w_out=np.ascontiguousarray(np.asarray(full["L0_attn_out_proj_w"], np.float32)),
        w_ffn=np.ascontiguousarray(np.asarray(full["L0_ffn_weight"], np.float32)),
        lnw_f=np.ascontiguousarray(np.asarray(full["L0_attn_ln_w"], np.float32).reshape(KC, 128).T),
        nrm_f=np.ascontiguousarray(np.asarray(full["L0_attn_norm_w"], np.float32).reshape(KC, 128).T),
        mup_f=np.ascontiguousarray(mup[H:H + CONV].reshape(NCH, 128).T),
        mup_d=np.ascontiguousarray(mup[H + CONV:].reshape(64, 1)),
        mup_z=np.ascontiguousarray(mup[0:H].reshape(1, H)),
        cw_f=np.ascontiguousarray(
            np.asarray(full["L0_attn_conv_w"], np.float32).reshape(NCH, 128, 4)
            .transpose(1, 0, 2).reshape(128, NCH * 4)),
        alog=np.ascontiguousarray(np.asarray(full["L0_attn_A_log"], np.float32).reshape(64, 1)),
        dtb=np.ascontiguousarray(np.asarray(full["L0_attn_dt_bias"], np.float32).reshape(64, 1)),
        drow=np.ascontiguousarray(np.asarray(full["L0_attn_D"], np.float32).reshape(1, 64)),
        ident=np.eye(128, dtype=np.float32),
        identb=np.eye(128, dtype=ml_dtypes.bfloat16),
        caus=np.ascontiguousarray(np.tril(np.ones((128, 128), np.float32)).T),
        ones1=np.ones((1, 128), np.float32),
    )
    in_maps = []
    for i in range(8):
        m = dict(common)
        m["x_own"] = np.ascontiguousarray(x[i * BL:(i + 1) * BL])
        m["myoff"] = np.array([[i * 128, i * 64]], np.int32)
        in_maps.append(m)
    return in_maps


def kernel(**inputs):
    nc = build(debug=False)
    in_maps = make_inputs(inputs)
    res = run_bass_kernel_spmd(nc, in_maps, core_ids=list(range(8)))
    parts = [res.results[i]["out"] for i in range(8)]
    return np.concatenate(parts, axis=0)[None].astype(np.float32)
